# version 03
# baseline (speedup 1.0000x reference)
import numpy as np
import concourse.bass as bass
import concourse.mybir as mybir
from concourse.bass_utils import run_bass_kernel_spmd
from contextlib import ExitStack

F32 = mybir.dt.float32
F16 = mybir.dt.float16
AF = mybir.ActivationFunctionType
ALU = mybir.AluOpType
AX = mybir.AxisListType

D = 1024
KC = 8
NCH = 16
T = 128
SEQH = NCH * T
EPS = 1e-5
SCL = 1.0 / 32.0


class Em:

    ENGS = ("te", "ve", "se", "gp", "sy")
    DMA_SEMS = {"sy": ["syd0", "syd1", "syd2", "syd3"],
                "se": ["sed0", "sed1", "sed2"],
                "gp": ["gpd0", "gpd1"]}
    SEMS = ("te", "ve", "se", "gp", "cc",
            "syd0", "syd1", "syd2", "syd3", "sed0", "sed1", "sed2", "gpd0", "gpd1")

    def __init__(self, strict=False):
        self.strict = strict
        self.ops = {e: [] for e in self.ENGS}
        self.cnt = {s: 0 for s in self.SEMS}
        self.wm = {e: {s: 0 for s in self.SEMS} for e in self.ENGS}
        self.bufs = {}
        self.dma_seq = {"sy": 0, "se": 0, "gp": 0}

    def op(self, eng, fn, reads=(), writes=(), inc=None, n_inc=1, is_dma=False):
        deps = []
        for b in reads:
            info = self.bufs.get(b)
            if info and info["ready"]:
                deps.append(info["ready"])
        for b in writes:
            info = self.bufs.get(b)
            if info:
                deps.extend(info["readers"])
                if info["ready"]:
                    deps.append(info["ready"])
        waits = []
        own = {"te": "te", "ve": "ve", "se": "se", "gp": "gp"}.get(eng)
        for sem, val in deps:
            if sem == own and not self.strict and not is_dma:
                continue
            if self.wm[eng][sem] < val:
                self.wm[eng][sem] = val
                waits.append((sem, val))
        if inc is None:
            assert eng in ("te", "ve", "se", "gp")
            inc = eng
            n_inc = 1
        self.cnt[inc] += n_inc
        stamp = (inc, self.cnt[inc])

        def closure(engine, sems, waits=tuple(waits), fn=fn, inc=inc, n_inc=n_inc):
            for sem, val in waits:
                engine.wait_ge(sems[sem], val)
            inst = fn(engine, sems)
            assert inst is not None, "emitted op must return its last instruction"
            inst.then_inc(sems[inc], n_inc)
            return inst

        self.ops[eng].append(closure)
        for b in reads:
            self.bufs.setdefault(b, {"ready": None, "readers": []})["readers"].append(stamp)
        for b in writes:
            self.bufs[b] = {"ready": stamp, "readers": []}
        return stamp


def build_program(ncores, strict=False, debug_spills=True):
    nc = bass.Bass("TRN2", target_bir_lowering=False,
                   detect_race_conditions=strict)
    spill_kind = dict(kind="ExternalOutput") if debug_spills else {}

    x_in = nc.dram_tensor("x", [SEQH, D], F32, kind="ExternalInput").ap()
    out_t = nc.dram_tensor("out", [SEQH, D], F32, kind="ExternalOutput").ap()
    wd = {n: nc.dram_tensor(n, [D, D], F32, kind="ExternalInput").ap()
          for n in ("wq", "wk", "wv", "wqa", "wka")}
    bd = {n: nc.dram_tensor(n, [1, D], F32, kind="ExternalInput").ap()
          for n in ("bq", "bk", "bv", "bqa", "bka", "bffn")}
    gamma_d = nc.dram_tensor("gamma", [128, KC], F32, kind="ExternalInput").ap()
    beta_d = nc.dram_tensor("beta", [128, KC], F32, kind="ExternalInput").ap()
    g1_d = nc.dram_tensor("g1", [32, 256], F32, kind="ExternalInput").ap()
    g2_d = nc.dram_tensor("g2", [16, 128], F32, kind="ExternalInput").ap()
    eye_d = nc.dram_tensor("eye", [128, 128], F16, kind="ExternalInput").ap()
    tri_d = nc.dram_tensor("tri", [128, 128], F16, kind="ExternalInput").ap()
    ltris_d = nc.dram_tensor("ltris", [16, 16], F16, kind="ExternalInput").ap()
    maskc_d = nc.dram_tensor("maskc", [8, 16], F16, kind="ExternalInput").ap()
    onesc_d = nc.dram_tensor("onesc", [128, 130], F16, kind="ExternalInput").ap()
    ud_d = nc.dram_tensor("ud", [32, 512], F32, kind="ExternalInput").ap()
    u2_d = nc.dram_tensor("u2", [16, 128], F32, kind="ExternalInput").ap()
    invt_d = nc.dram_tensor("invt", [128, NCH], F32, kind="ExternalInput").ap()
    selc_d = nc.dram_tensor("selc", [16, NCH * 128], F16, kind="ExternalInput").ap()

    n2t_sp = nc.dram_tensor("n2t_sp", [NCH, 128, 2 * D], F16, **spill_kind).ap()
    a1_sp = nc.dram_tensor("a1_sp", [NCH, 128, 2 * D], F16, **spill_kind).ap()
    a2_sp = nc.dram_tensor("a2_sp", [NCH, 128, 2 * D], F16, **spill_kind).ap()
    k_sp = nc.dram_tensor("k_sp", [NCH, 128, D], F32, **spill_kind).ap()
    v_sp = nc.dram_tensor("v_sp", [NCH, 128, D], F32, **spill_kind).ap()
    cc1_in = nc.dram_tensor("cc1_in", [1, D], F32)
    cc1_out = nc.dram_tensor("cc1_out", [ncores, D], F32)
    cc2_in = nc.dram_tensor("cc2_in", [1, D], F32)
    cc2_out = nc.dram_tensor("cc2_out", [ncores, D], F32)

    es = ExitStack()
    sbuf = lambda name, shape, dt: es.enter_context(nc.sbuf_tensor(name, shape, dt))
    psum = lambda name, shape, dt: es.enter_context(nc.psum_tensor(name, shape, dt))

    eye = sbuf("eye_s", [128, 128], F16)
    tri = sbuf("tri_s", [128, 128], F16)
    ltris = sbuf("ltris_s", [16, 16], F16)
    maskc = sbuf("maskc_s", [8, 16], F16)
    onesc = sbuf("onesc_s", [128, 130], F16)
    gamma = sbuf("gamma_s", [128, KC], F32)
    beta = sbuf("beta_s", [128, KC], F32)
    invt = sbuf("invt_s", [128, NCH], F32)
    g1 = sbuf("g1_s", [32, 256], F32)
    g2 = sbuf("g2_s", [16, 128], F32)
    ud = sbuf("ud_s", [32, 512], F32)
    u2 = sbuf("u2_s", [16, 128], F32)
    one1 = sbuf("one1_s", [1, 1], F32)
    selc = sbuf("selc_s", [16, NCH * 128], F16)

    wph = sbuf("wph", [128, 2 * KC * D], F16)
    wpl = sbuf("wpl", [128, 2 * KC * D], F16)
    wstg = sbuf("wstg", [128, 2 * D], F32)
    bq_rep = sbuf("bq_rep", [128, D], F32)
    bk_rep = sbuf("bk_rep", [128, D], F32)
    bv_rep = sbuf("bv_rep", [128, D], F32)
    bz_rep = sbuf("bz_rep", [128, D], F32)
    bf_rep = sbuf("bf_rep", [128, D], F32)
    brow = sbuf("brow", [1, 4 * D], F32)
    browp = sbuf("browp", [1, 2 * D], F16)
    t_tile = sbuf("t_tile", [16, D], F32)
    t_pair = sbuf("t_pair", [16, 2 * D], F16)
    c_tile = sbuf("c_tile", [16, D], F32)
    c_pair = sbuf("c_pair", [16, 2 * D], F16)
    g_tile = sbuf("g_tile", [8, D], F32)
    g_pair = sbuf("g_pair", [8, 2 * D], F16)
    tot_row = sbuf("tot_row", [1, D], F32)
    trow = tot_row

    xb = [sbuf(f"xb{i}", [128, D], F32) for i in range(2)]
    bA = [sbuf(f"bA{i}", [128, D], F32) for i in range(4)]
    bH = [sbuf(f"bH{i}", [128, 2 * D], F16) for i in range(6)]
    scr = sbuf("scr_s", [128, D], F32)
    facc = scr[:, 0:512]
    fprod = scr[:, 512:1024]
    vab = sbuf("vab", [32, 512], F32)
    sm = sbuf("sm", [128, 64], F32)

    P = [psum(f"P{i}", [128, 512], F32) for i in range(6)]
    PTh = psum("PTh", [128, D], F16)
    PTl = psum("PTl", [128, D], F16)

    sem_handles = {s: es.enter_context(nc.semaphore("sem_" + s)) for s in Em.SEMS}
    em = Em(strict=strict)

    ONES_COL = onesc[:, 0:1]
    ONES_ROW = onesc[0:1, 1:129]
    ONES16 = onesc[0:16, 0:1]

    def dma(queue, out, in_, reads, writes):
        ring = Em.DMA_SEMS[queue]
        qsem = ring[em.dma_seq[queue] % len(ring)]
        em.dma_seq[queue] += 1
        prev = em.cnt[qsem]

        def fn(e, s, out=out, in_=in_, qsem=qsem, prev=prev):
            if prev:
                e.wait_ge(s[qsem], prev)
            return e.dma_start(out=out, in_=in_)

        return em.op(queue, fn, reads=reads, writes=writes, inc=qsem, n_inc=16,
                     is_dma=True)

    def split(src_ap, hi_ap, lo_ap, rname, wname, cast_eng="ve", sub_eng="ve"):
        if cast_eng == "se":
            em.op("se", lambda e, s: e.copy(hi_ap, src_ap), reads=[rname], writes=[wname + "_h"])
        else:
            em.op(cast_eng, lambda e, s: e.tensor_copy(hi_ap, src_ap),
                  reads=[rname], writes=[wname + "_h"])
        em.op(sub_eng, lambda e, s: e.tensor_sub(lo_ap, src_ap, hi_ap),
              reads=[rname, wname + "_h"], writes=[wname + "_l"])

    def mm3(ps0, ps1, st_h, st_l, w_h, w_l, reads, psname):
        def fn(eng, sems):
            last = None
            for kc in range(KC):
                for half, ps in ((0, ps0), (1, ps1)):
                    last = eng.matmul(ps, st_h(kc), w_h(kc, half),
                                      start=(kc == 0), stop=False)
                    last = eng.matmul(ps, st_h(kc), w_l(kc, half),
                                      start=False, stop=False)
                    last = eng.matmul(ps, st_l(kc), w_h(kc, half),
                                      start=False, stop=(kc == KC - 1))
            return last
        return em.op("te", fn, reads=list(reads), writes=[psname])

    def transp(pair_buf, pname):
        def fn(eng, sems):
            last = None
            for kc in range(KC):
                last = eng.matmul(PTh[:, kc * 128:(kc + 1) * 128],
                                  pair_buf[:, kc * 128:(kc + 1) * 128], eye[:],
                                  is_transpose=True, start=True, stop=True,
                                  skip_group_check=True)
            for kc in range(KC):
                last = eng.matmul(PTl[:, kc * 128:(kc + 1) * 128],
                                  pair_buf[:, D + kc * 128:D + (kc + 1) * 128], eye[:],
                                  is_transpose=True, start=True, stop=True,
                                  skip_group_check=True)
            return last
        return em.op("te", fn, reads=[pname + "_h", pname + "_l", "eye"], writes=["psTT"])

    def evacT(dst_buf, dname):
        em.op("se", lambda e, s: e.copy(dst_buf[:, 0:D], PTh[:]),
              reads=["psTT"], writes=[dname + "_h"])
        em.op("se", lambda e, s: e.copy(dst_buf[:, D:2 * D], PTl[:]),
              reads=["psTT"], writes=[dname + "_l"])

    def repl_bias(src_row_ap, rname, dst_tile, dst_name, scale=None):
        if scale is None:
            em.op("ve", lambda e, s: e.tensor_copy(tot_row[:], src_row_ap),
                  reads=[rname], writes=["tot_row"])
        else:
            em.op("ve", lambda e, s: e.tensor_scalar(tot_row[:], src_row_ap, scale, None, ALU.mult),
                  reads=[rname], writes=["tot_row"])
        split(tot_row[0:1, :], browp[0:1, 0:D], browp[0:1, D:2 * D], "tot_row", "browp")
        def fn(eng, sems):
            eng.matmul(P[4][:, :], ONES_ROW, browp[0:1, 0:512], start=True, stop=False)
            eng.matmul(P[4][:, :], ONES_ROW, browp[0:1, D:D + 512], start=False, stop=True)
            eng.matmul(P[5][:, :], ONES_ROW, browp[0:1, 512:1024], start=True, stop=False)
            return eng.matmul(P[5][:, :], ONES_ROW, browp[0:1, D + 512:2 * D],
                              start=False, stop=True)
        em.op("te", fn, reads=["browp_h", "browp_l", "onesc"], writes=["ps45"])
        em.op("ve", lambda e, s: e.tensor_copy(dst_tile[:, 0:512], P[4][:, :]),
              reads=["ps45"], writes=[dst_name])
        em.op("ve", lambda e, s: e.tensor_copy(dst_tile[:, 512:1024], P[5][:, :]),
              reads=["ps45"], writes=[dst_name])

    def prep_folded(w_dram, slot, brow_sl, brow_name, brep_tile, brep_name):
        base = slot * KC * D
        for kc in range(KC):
            stg = wstg[:, (kc % 2) * D:(kc % 2 + 1) * D]
            sname = f"wstg{kc % 2}"
            dma("sy", stg, w_dram[kc * 128:(kc + 1) * 128, :], [sname], [sname])
            def bf(eng, sems, kc=kc, stg=stg):
                eng.matmul(P[4][0:1, :], beta[:, kc:kc + 1], stg[:, 0:512],
                           start=(kc == 0), stop=False)
                return eng.matmul(P[5][0:1, :], beta[:, kc:kc + 1], stg[:, 512:1024],
                                  start=(kc == 0), stop=False)
            em.op("te", bf, reads=[sname, "beta"], writes=["ps45"])
            em.op("ve", lambda e, s, kc=kc, stg=stg: e.tensor_scalar(
                stg, stg, gamma[:, kc:kc + 1], None, ALU.mult),
                reads=[sname, "gamma"], writes=[sname])
            em.op("se", lambda e, s, kc=kc, stg=stg: e.copy(
                wph[:, base + kc * D: base + (kc + 1) * D], stg),
                reads=[sname], writes=[f"wslot{slot}"])
            em.op("gp", lambda e, s, kc=kc, stg=stg: e.tensor_sub(
                wpl[:, base + kc * D: base + (kc + 1) * D], stg,
                wph[:, base + kc * D: base + (kc + 1) * D]),
                reads=[sname, f"wslot{slot}"], writes=[f"wslot{slot}"])
        def bf2(eng, sems):
            eng.matmul(P[4][0:1, :], one1[:], brow_sl[:, 0:512], start=False, stop=True)
            return eng.matmul(P[5][0:1, :], one1[:], brow_sl[:, 512:1024], start=False, stop=True)
        em.op("te", bf2, reads=[brow_name, "one1"], writes=["ps45"])
        em.op("se", lambda e, s: e.copy(tot_row[:, 0:512], P[4][0:1, :]),
              reads=["ps45"], writes=["tot_row"])
        em.op("se", lambda e, s: e.copy(tot_row[:, 512:1024], P[5][0:1, :]),
              reads=["ps45"], writes=["tot_row"])
        repl_bias(tot_row[0:1, :], "tot_row", brep_tile, brep_name)

    def prep_scaled(w_dram, slot, scale):
        base = slot * KC * D
        for kc in range(KC):
            stg = wstg[:, (kc % 2) * D:(kc % 2 + 1) * D]
            sname = f"wstg{kc % 2}"
            dma("sy", stg, w_dram[kc * 128:(kc + 1) * 128, :], [sname], [sname])
            em.op("ve", lambda e, s, stg=stg: e.tensor_scalar(stg, stg, scale, None, ALU.mult),
                  reads=[sname], writes=[sname])
            em.op("se", lambda e, s, kc=kc, stg=stg: e.copy(
                wph[:, base + kc * D: base + (kc + 1) * D], stg),
                reads=[sname], writes=[f"wslot{slot}"])
            em.op("gp", lambda e, s, kc=kc, stg=stg: e.tensor_sub(
                wpl[:, base + kc * D: base + (kc + 1) * D], stg,
                wph[:, base + kc * D: base + (kc + 1) * D]),
                reads=[sname, f"wslot{slot}"], writes=[f"wslot{slot}"])

    def ln_stats(src_ap, src_name, col):
        mu = sm[:, col + 2:col + 3]
        ssq = sm[:, col + 3:col + 4]
        va = sm[:, col + 4:col + 5]
        tmp = sm[:, col + 5:col + 6]
        rho = sm[:, col:col + 1]
        bet = sm[:, col + 1:col + 2]
        nm = f"sm{col}"
        em.op("se", lambda e, s: e.activation(scr[:], src_ap, AF.Identity,
                                              bias=0.0, scale=1.0 / D, accum_out=mu),
              reads=[src_name], writes=["scr", nm + "_mu"])
        em.op("se", lambda e, s: e.activation(scr[:], src_ap, AF.Square,
                                              bias=0.0, scale=1.0, accum_out=ssq),
              reads=[src_name], writes=["scr", nm + "_ssq"])
        em.op("ve", lambda e, s: e.tensor_scalar(va, ssq, 1.0 / D, None, ALU.mult),
              reads=[nm + "_ssq"], writes=[nm + "_va"])
        em.op("ve", lambda e, s: e.scalar_tensor_tensor(tmp, mu, -1.0, mu, ALU.mult, ALU.mult),
              reads=[nm + "_mu"], writes=[nm + "_tmp"])
        em.op("ve", lambda e, s: e.tensor_tensor(va, va, tmp, ALU.add),
              reads=[nm + "_va", nm + "_tmp"], writes=[nm + "_va"])
        em.op("ve", lambda e, s: e.tensor_scalar(tmp, va, 1.0, EPS, ALU.mult, ALU.add),
              reads=[nm + "_va"], writes=[nm + "_tmp"])
        em.op("ve", lambda e, s: e.reciprocal(rho, tmp),
              reads=[nm + "_tmp"], writes=[nm + "_rho"])
        em.op("ve", lambda e, s: e.tensor_tensor(va, va, rho, ALU.mult),
              reads=[nm + "_va", nm + "_rho"], writes=[nm + "_va"])
        em.op("ve", lambda e, s: e.tensor_scalar(va, va, 1.0, EPS, ALU.mult, ALU.add),
              reads=[nm + "_va"], writes=[nm + "_va"])
        em.op("ve", lambda e, s: e.tensor_tensor(tmp, tmp, va, ALU.mult),
              reads=[nm + "_va", nm + "_tmp"], writes=[nm + "_tmp"])
        em.op("ve", lambda e, s: e.reciprocal(tmp, tmp),
              reads=[nm + "_tmp"], writes=[nm + "_tmp"])
        em.op("se", lambda e, s: e.activation(rho, tmp, AF.Sqrt, bias=0.0, scale=1.0),
              reads=[nm + "_tmp"], writes=[nm + "_rho"])
        em.op("ve", lambda e, s: e.scalar_tensor_tensor(bet, mu, -1.0, rho, ALU.mult, ALU.mult),
              reads=[nm + "_rho", nm + "_mu"], writes=[nm + "_bet"])
        return rho, bet, nm

    def tri_cumsum(a_buf, a_name, cp, c, psname):
        sel = selc[:, c * 128:(c + 1) * 128]
        def fn(eng, sems, a_buf=a_buf, sel=sel):
            eng.matmul(P[0][:, :], tri[:], a_buf[:, 0:512], start=True, stop=False)
            eng.matmul(P[0][:, :], tri[:], a_buf[:, D:D + 512], start=False, stop=False)
            eng.matmul(P[0][:, :], sel, cp[0:16, 0:512], start=False, stop=False)
            eng.matmul(P[0][:, :], sel, cp[0:16, D:D + 512], start=False, stop=True)
            eng.matmul(P[1][:, :], tri[:], a_buf[:, 512:1024], start=True, stop=False)
            eng.matmul(P[1][:, :], tri[:], a_buf[:, D + 512:2 * D], start=False, stop=False)
            eng.matmul(P[1][:, :], sel, cp[0:16, 512:1024], start=False, stop=False)
            return eng.matmul(P[1][:, :], sel, cp[0:16, D + 512:2 * D],
                              start=False, stop=True)
        return em.op("te", fn, reads=[a_name + "_h", a_name + "_l", "c_pair_h", "c_pair_l",
                                      "tri", "selc"],
                     writes=[psname])

    def colsum(a_buf, a_name, c, rowname):
        def fn(eng, sems, a_buf=a_buf):
            eng.matmul(P[4][0:1, :], ONES_COL, a_buf[:, 0:512], start=True, stop=False)
            eng.matmul(P[4][0:1, :], ONES_COL, a_buf[:, D:D + 512], start=False, stop=True)
            eng.matmul(P[5][0:1, :], ONES_COL, a_buf[:, 512:1024], start=True, stop=False)
            return eng.matmul(P[5][0:1, :], ONES_COL, a_buf[:, D + 512:2 * D],
                              start=False, stop=True)
        em.op("te", fn, reads=[a_name + "_h", a_name + "_l", "onesc"], writes=["ps45"])
        em.op("se", lambda e, s: e.copy(trow[:, 0:512], P[4][0:1, :]),
              reads=["ps45"], writes=["tot_row"])
        em.op("se", lambda e, s: e.copy(trow[:, 512:1024], P[5][0:1, :]),
              reads=["ps45"], writes=["tot_row"])
        dma("gp", t_tile[c:c + 1, :], trow[:], ["tot_row", "t_tile"], ["t_tile"])

    def tot_and_trigger(cc_in_t, cc_out_t, ccb_name, cco_name):
        split(t_tile[0:16, :], t_pair[0:16, 0:D], t_pair[0:16, D:2 * D], "t_tile", "t_pair")
        def fn(eng, sems):
            eng.matmul(P[4][0:1, :], ONES16, t_pair[0:16, 0:512], start=True, stop=False)
            eng.matmul(P[4][0:1, :], ONES16, t_pair[0:16, D:D + 512], start=False, stop=True)
            eng.matmul(P[5][0:1, :], ONES16, t_pair[0:16, 512:1024], start=True, stop=False)
            return eng.matmul(P[5][0:1, :], ONES16, t_pair[0:16, D + 512:2 * D],
                              start=False, stop=True)
        em.op("te", fn, reads=["t_pair_h", "t_pair_l", "onesc"], writes=["ps45"])
        em.op("se", lambda e, s: e.copy(tot_row[:, 0:512], P[4][0:1, :]),
              reads=["ps45"], writes=["tot_row"])
        em.op("se", lambda e, s: e.copy(tot_row[:, 512:1024], P[5][0:1, :]),
              reads=["ps45"], writes=["tot_row"])
        dma("gp", cc_in_t.ap(), tot_row[:], ["tot_row"], [ccb_name])
        def coll(eng, sems):
            return eng.collective_compute(
                "AllGather", ALU.bypass, replica_groups=[list(range(ncores))],
                ins=[cc_in_t.ap().opt()], outs=[cc_out_t.ap().opt()])
        em.op("gp", coll, reads=[ccb_name], writes=[cco_name], inc="cc", n_inc=1)

    def scan(cco_name, cc_out_t):
        dma("gp", g_tile[0:ncores, :], cc_out_t.ap(), [cco_name, "g_tile"], ["g_tile"])
        split(g_tile[0:ncores, :], g_pair[0:ncores, 0:D], g_pair[0:ncores, D:2 * D],
              "g_tile", "g_pair")
        def fn(eng, sems):
            eng.matmul(P[4][0:16, :], ltris[:], t_pair[0:16, 0:512], start=True, stop=False)
            eng.matmul(P[4][0:16, :], ltris[:], t_pair[0:16, D:D + 512], start=False, stop=False)
            eng.matmul(P[4][0:16, :], maskc[0:ncores, :], g_pair[0:ncores, 0:512],
                       start=False, stop=False)
            eng.matmul(P[4][0:16, :], maskc[0:ncores, :], g_pair[0:ncores, D:D + 512],
                       start=False, stop=True)
            eng.matmul(P[5][0:16, :], ltris[:], t_pair[0:16, 512:1024], start=True, stop=False)
            eng.matmul(P[5][0:16, :], ltris[:], t_pair[0:16, D + 512:2 * D], start=False, stop=False)
            eng.matmul(P[5][0:16, :], maskc[0:ncores, :], g_pair[0:ncores, 512:1024],
                       start=False, stop=False)
            return eng.matmul(P[5][0:16, :], maskc[0:ncores, :], g_pair[0:ncores, D + 512:2 * D],
                              start=False, stop=True)
        em.op("te", fn, reads=["g_pair_h", "g_pair_l", "t_pair_h", "t_pair_l",
                               "maskc", "ltris"], writes=["ps45"])
        em.op("se", lambda e, s: e.copy(c_tile[0:16, 0:512], P[4][0:16, :]),
              reads=["ps45"], writes=["c_tile"])
        em.op("se", lambda e, s: e.copy(c_tile[0:16, 512:1024], P[5][0:16, :]),
              reads=["ps45"], writes=["c_tile"])
        split(c_tile[0:16, :], c_pair[0:16, 0:D], c_pair[0:16, D:2 * D], "c_tile", "c_pair")

    W0h = lambda kc, half: wph[:, kc * D + half * 512: kc * D + (half + 1) * 512]
    W0l = lambda kc, half: wpl[:, kc * D + half * 512: kc * D + (half + 1) * 512]
    W1h = lambda kc, half: wph[:, (KC + kc) * D + half * 512: (KC + kc) * D + (half + 1) * 512]
    W1l = lambda kc, half: wpl[:, (KC + kc) * D + half * 512: (KC + kc) * D + (half + 1) * 512]

    dma("sy", eye[:], eye_d, [], ["eye"])
    dma("sy", tri[:], tri_d, [], ["tri"])
    dma("sy", ltris[:], ltris_d, [], ["ltris"])
    dma("sy", maskc[:], maskc_d, [], ["maskc"])
    dma("sy", onesc[:], onesc_d, [], ["onesc"])
    dma("sy", gamma[:], gamma_d, [], ["gamma"])
    dma("sy", beta[:], beta_d, [], ["beta"])
    dma("sy", invt[:], invt_d, [], ["invt"])
    dma("sy", selc[:], selc_d, [], ["selc"])
    dma("sy", g1[:], g1_d, [], ["g1"])
    dma("sy", g2[:], g2_d, [], ["g2"])
    dma("sy", ud[:], ud_d, [], ["ud"])
    dma("sy", u2[:], u2_d, [], ["u2"])
    dma("sy", brow[:, 0:D], bd["bq"], [], ["brow0"])
    dma("sy", brow[:, D:2 * D], bd["bk"], [], ["brow1"])
    dma("sy", brow[:, 2 * D:3 * D], bd["bv"], [], ["brow2"])
    dma("sy", brow[:, 3 * D:4 * D], bd["bqa"], [], ["brow3"])
    em.op("gp", lambda e, s: e.memset(one1[:], 1.0), reads=[], writes=["one1"])

    prep_folded(wd["wq"], 0, brow[0:1, 0:D], "brow0", bq_rep, "bq_rep")
    prep_scaled(wd["wqa"], 1, SCL)
    repl_bias(brow[0:1, 3 * D:4 * D], "brow3", bz_rep, "bz_rep", scale=SCL)

    for c in range(NCH):
        xc, xn = xb[c % 2], f"xb{c % 2}"
        n2t_buf, n2t_nm = bH[1 + c % 2], f"bH{1 + c % 2}"
        dma("sy", xc[:], x_in[c * T:(c + 1) * T, :], [xn], [xn])
        rho, bet, snm = ln_stats(xc[:], xn, 0 if c % 2 == 0 else 8)
        em.op("ve", lambda e, s, xc=xc, rho=rho, bet=bet: e.tensor_scalar(
            bA[0][:], xc[:], rho, bet, ALU.mult, ALU.add),
            reads=[snm + "_rho", snm + "_bet", xn], writes=["bA0"])
        split(bA[0][:], bH[0][:, 0:D], bH[0][:, D:2 * D], "bA0", "bH0")
        transp(bH[0], "bH0")
        evacT(n2t_buf, n2t_nm)
        dma("se", n2t_sp[c], n2t_buf[:], [n2t_nm + "_h", n2t_nm + "_l"], [f"n2tsp{c}"])
        mm3(P[0][:, :], P[1][:, :],
            lambda kc, b=n2t_buf: b[:, kc * 128:(kc + 1) * 128],
            lambda kc, b=n2t_buf: b[:, D + kc * 128:D + (kc + 1) * 128],
            W0h, W0l, [n2t_nm + "_h", n2t_nm + "_l", "wslot0"], "ps01")
        em.op("ve", lambda e, s: e.tensor_tensor(bA[1][:, 0:512], P[0][:, :],
                                                 bq_rep[:, 0:512], ALU.add),
              reads=["ps01", "bq_rep"], writes=["bA1"])
        em.op("ve", lambda e, s: e.tensor_tensor(bA[1][:, 512:1024], P[1][:, :],
                                                 bq_rep[:, 512:1024], ALU.add),
              reads=["ps01"], writes=["bA1"])
        split(bA[1][:], bH[3][:, 0:D], bH[3][:, D:2 * D], "bA1", "bH3")
        transp(bH[3], "bH3")
        evacT(bH[4], "bH4")
        mm3(P[2][:, :], P[3][:, :],
            lambda kc: bH[4][:, kc * 128:(kc + 1) * 128],
            lambda kc: bH[4][:, D + kc * 128:D + (kc + 1) * 128],
            W1h, W1l, ["bH4_h", "bH4_l", "wslot1"], "ps23")
        em.op("ve", lambda e, s: e.tensor_tensor(bA[2][:, 0:512], P[2][:, :],
                                                 bz_rep[:, 0:512], ALU.add),
              reads=["ps23", "bz_rep"], writes=["bA2"])
        em.op("ve", lambda e, s: e.tensor_tensor(bA[2][:, 512:1024], P[3][:, :],
                                                 bz_rep[:, 512:1024], ALU.add),
              reads=["ps23"], writes=["bA2"])
        m1, nm1 = sm[:, 16:17], sm[:, 17:18]
        s1v, r1 = sm[:, 18:19], sm[:, 19:20]
        em.op("ve", lambda e, s: e.reduce_max(m1, bA[2][:], axis=AX.X),
              reads=["bA2"], writes=["sm16"])
        em.op("ve", lambda e, s: e.tensor_scalar(nm1, m1, -1.0, None, ALU.mult),
              reads=["sm16"], writes=["sm17"])
        em.op("se", lambda e, s: e.activation(bA[3][:], bA[2][:], AF.Exp, bias=nm1,
                                              scale=1.0, accum_out=s1v),
              reads=["bA2", "sm17"], writes=["bA3", "sm18"])
        em.op("ve", lambda e, s: e.reciprocal(r1, s1v), reads=["sm18"], writes=["sm19"])
        em.op("ve", lambda e, s: e.scalar_tensor_tensor(bA[0][:], bA[3][:], r1, bA[1][:],
                                                        ALU.mult, ALU.mult),
              reads=["bA3", "sm19", "bA1"], writes=["bA0"])
        split(bA[0][:], bH[5][:, 0:D], bH[5][:, D:2 * D], "bA0", "bH5")
        dma("se", a1_sp[c], bH[5][:], ["bH5_h", "bH5_l"], [f"a1sp{c}"])
        colsum(bH[5], "bH5", c, "t1")
    tot_and_trigger(cc1_in, cc1_out, "cc1b", "cc1o")

    prep_folded(wd["wk"], 0, brow[0:1, D:2 * D], "brow1", bk_rep, "bk_rep")
    prep_folded(wd["wv"], 1, brow[0:1, 2 * D:3 * D], "brow2", bv_rep, "bv_rep")
    for c in range(NCH):
        n2t_buf, n2t_nm = bH[1 + c % 2], f"bH{1 + c % 2}"
        dma("sy", n2t_buf[:], n2t_sp[c], [n2t_nm + "_h", n2t_nm + "_l", f"n2tsp{c}"],
            [n2t_nm + "_h", n2t_nm + "_l"])
        mm3(P[0][:, :], P[1][:, :],
            lambda kc, b=n2t_buf: b[:, kc * 128:(kc + 1) * 128],
            lambda kc, b=n2t_buf: b[:, D + kc * 128:D + (kc + 1) * 128],
            W0h, W0l, [n2t_nm + "_h", n2t_nm + "_l", "wslot0"], "ps01")
        em.op("ve", lambda e, s: e.tensor_tensor(bA[0][:, 0:512], P[0][:, :],
                                                 bk_rep[:, 0:512], ALU.add),
              reads=["ps01", "bk_rep"], writes=["bA0"])
        em.op("ve", lambda e, s: e.tensor_tensor(bA[0][:, 512:1024], P[1][:, :],
                                                 bk_rep[:, 512:1024], ALU.add),
              reads=["ps01"], writes=["bA0"])
        dma("se", k_sp[c], bA[0][:], ["bA0"], [f"ksp{c}"])
        mm3(P[2][:, :], P[3][:, :],
            lambda kc, b=n2t_buf: b[:, kc * 128:(kc + 1) * 128],
            lambda kc, b=n2t_buf: b[:, D + kc * 128:D + (kc + 1) * 128],
            W1h, W1l, [n2t_nm + "_h", n2t_nm + "_l", "wslot1"], "ps23")
        em.op("ve", lambda e, s: e.tensor_tensor(bA[1][:, 0:512], P[2][:, :],
                                                 bv_rep[:, 0:512], ALU.add),
              reads=["ps23", "bv_rep"], writes=["bA1"])
        em.op("ve", lambda e, s: e.tensor_tensor(bA[1][:, 512:1024], P[3][:, :],
                                                 bv_rep[:, 512:1024], ALU.add),
              reads=["ps23"], writes=["bA1"])
        dma("se", v_sp[c], bA[1][:], ["bA1"], [f"vsp{c}"])
    scan("cc1o", cc1_out)

    prep_scaled(wd["wka"], 0, SCL)
    dma("sy", brow[:, 3 * D:4 * D], bd["bka"], ["brow3"], ["brow3"])
    repl_bias(brow[0:1, 3 * D:4 * D], "brow3", bz_rep, "bz_rep", scale=SCL)
    dma("sy", brow[:, 0:D], bd["bffn"], ["brow0"], ["brow0f"])

    ffn_base = KC * D

    def build_wt(m, g1c0, g2c0, col0, negate):
        for ci in range(4):
            em.op("ve", lambda e, s, ci=ci, g1c0=g1c0: e.tensor_copy(
                vab[:, :].rearrange("p (b e) -> p b e", b=32),
                g1[:, (g1c0 + ci) * 32:(g1c0 + ci + 1) * 32][:, :, None]
                .to_broadcast([32, 32, 16])),
                reads=["g1", "vab"], writes=["vab"])
            em.op("te", lambda e, s, m=m: e.matmul(
                P[4][:, :], ud[:, m * 128:(m + 1) * 128], vab[:, :], start=True, stop=True),
                reads=["vab", "ud"], writes=["ps45"])
            em.op("ve", lambda e, s: e.tensor_copy(fprod, P[4][:, :]),
                  reads=["ps45"], writes=["scr"])
            em.op("ve", lambda e, s, ci=ci, g2c0=g2c0: e.tensor_copy(
                vab[0:16, :].rearrange("p (b e) -> p b e", b=32),
                g2[:, (g2c0 + ci) * 16:(g2c0 + ci + 1) * 16][:, None, :]
                .to_broadcast([16, 32, 16])),
                reads=["g2", "vab"], writes=["vab"])
            em.op("te", lambda e, s: e.matmul(P[5][:, :], u2[:, :], vab[0:16, :],
                                              start=True, stop=True),
                  reads=["vab", "u2"], writes=["ps45"])
            if ci == 0:
                em.op("ve", lambda e, s: e.scalar_tensor_tensor(
                    facc, fprod, -1.0 if negate else 1.0, P[5][:, :],
                    ALU.mult, ALU.mult),
                    reads=["ps45", "scr"], writes=["scr"])
            else:
                em.op("ve", lambda e, s: e.scalar_tensor_tensor(
                    fprod, fprod, -1.0 if negate else 1.0, P[5][:, :],
                    ALU.mult, ALU.mult),
                    reads=["ps45", "scr"], writes=["scr"])
                em.op("ve", lambda e, s: e.tensor_tensor(facc, facc, fprod, ALU.add),
                      reads=["scr"], writes=["scr"])
        em.op("se", lambda e, s, col0=col0: e.copy(wph[:, col0:col0 + 512], facc),
              reads=["scr"], writes=["wslot1"])
        em.op("gp", lambda e, s, col0=col0: e.tensor_sub(
            wpl[:, col0:col0 + 512], facc, wph[:, col0:col0 + 512]),
            reads=["scr", "wslot1"], writes=["wslot1"])

    ffn_jobs = []
    for m in range(4):
        ffn_jobs.append((m, 0, 0, ffn_base + m * D, False))
        ffn_jobs.append((m, 4, 4, ffn_base + m * D + 512, False))
    for m in range(4):
        ffn_jobs.append((m, 4, 4, ffn_base + (m + 4) * D, True))
        ffn_jobs.append((m, 0, 0, ffn_base + (m + 4) * D + 512, False))

    for c in range(NCH):
        a1b, a1nm = bH[1 + c % 2], f"bH{1 + c % 2}"
        kb, knm = xb[c % 2], f"xb{c % 2}"
        dma("sy", a1b[:], a1_sp[c], [a1nm + "_h", a1nm + "_l", f"a1sp{c}"],
            [a1nm + "_h", a1nm + "_l"])
        dma("sy", kb[:], k_sp[c], [knm, f"ksp{c}"], [knm])
        tri_cumsum(a1b, a1nm, c_pair, c, "ps01")
        em.op("ve", lambda e, s, kb=kb, c=c: e.scalar_tensor_tensor(
            bA[0][:, 0:512], P[0][:, :], invt[:, c:c + 1], kb[:, 0:512], ALU.mult, ALU.mult),
            reads=["ps01", knm, "invt"], writes=["bA0"])
        em.op("ve", lambda e, s, kb=kb, c=c: e.scalar_tensor_tensor(
            bA[0][:, 512:1024], P[1][:, :], invt[:, c:c + 1], kb[:, 512:1024],
            ALU.mult, ALU.mult),
            reads=["ps01", knm], writes=["bA0"])
        split(bA[0][:], bH[3][:, 0:D], bH[3][:, D:2 * D], "bA0", "bH3")
        transp(bH[3], "bH3")
        evacT(bH[4], "bH4")
        mm3(P[2][:, :], P[3][:, :],
            lambda kc: bH[4][:, kc * 128:(kc + 1) * 128],
            lambda kc: bH[4][:, D + kc * 128:D + (kc + 1) * 128],
            W0h, W0l, ["bH4_h", "bH4_l", "wslot0"], "ps23")
        em.op("ve", lambda e, s: e.tensor_tensor(bA[2][:, 0:512], P[2][:, :],
                                                 bz_rep[:, 0:512], ALU.add),
              reads=["ps23", "bz_rep"], writes=["bA2"])
        em.op("ve", lambda e, s: e.tensor_tensor(bA[2][:, 512:1024], P[3][:, :],
                                                 bz_rep[:, 512:1024], ALU.add),
              reads=["ps23"], writes=["bA2"])
        m2, nm2 = sm[:, 20:21], sm[:, 21:22]
        s2v, r2 = sm[:, 22:23], sm[:, 23:24]
        em.op("ve", lambda e, s: e.reduce_max(m2, bA[2][:], axis=AX.X),
              reads=["bA2"], writes=["sm20"])
        em.op("ve", lambda e, s: e.tensor_scalar(nm2, m2, -1.0, None, ALU.mult),
              reads=["sm20"], writes=["sm21"])
        em.op("se", lambda e, s: e.activation(bA[3][:], bA[2][:], AF.Exp, bias=nm2,
                                              scale=1.0, accum_out=s2v),
              reads=["bA2", "sm21"], writes=["bA3", "sm22"])
        em.op("ve", lambda e, s: e.reciprocal(r2, s2v), reads=["sm22"], writes=["sm23"])
        em.op("ve", lambda e, s: e.scalar_tensor_tensor(bA[1][:], bA[3][:], r2, bA[0][:],
                                                        ALU.mult, ALU.mult),
              reads=["bA3", "sm23", "bA0"], writes=["bA1"])
        split(bA[1][:], bH[5][:, 0:D], bH[5][:, D:2 * D], "bA1", "bH5")
        dma("se", a2_sp[c], bH[5][:], ["bH5_h", "bH5_l"], [f"a2sp{c}"])
        colsum(bH[5], "bH5", c, "t2")
        if c < len(ffn_jobs):
            build_wt(*ffn_jobs[c])
    for j in ffn_jobs[NCH:]:
        build_wt(*j)
    repl_bias(brow[0:1, 0:D], "brow0f", bf_rep, "bf_rep")
    tot_and_trigger(cc2_in, cc2_out, "cc2b", "cc2o")
    scan("cc2o", cc2_out)

    FWh = lambda kc, half: wph[:, ffn_base + kc * D + half * 512: ffn_base + kc * D + (half + 1) * 512]
    FWl = lambda kc, half: wpl[:, ffn_base + kc * D + half * 512: ffn_base + kc * D + (half + 1) * 512]
    for c in range(NCH):
        a2b, a2nm = bH[1 + c % 2], f"bH{1 + c % 2}"
        vb, vnm = xb[c % 2], f"xb{c % 2}"
        dma("sy", a2b[:], a2_sp[c], [a2nm + "_h", a2nm + "_l", f"a2sp{c}"],
            [a2nm + "_h", a2nm + "_l"])
        dma("sy", vb[:], v_sp[c], [vnm, f"vsp{c}"], [vnm])
        dma("sy", bA[3][:], x_in[c * T:(c + 1) * T, :], ["bA3"], ["bA3"])
        tri_cumsum(a2b, a2nm, c_pair, c, "ps01")
        em.op("ve", lambda e, s, vb=vb, c=c: e.scalar_tensor_tensor(
            bA[0][:, 0:512], P[0][:, :], invt[:, c:c + 1], vb[:, 0:512], ALU.mult, ALU.mult),
            reads=["ps01", vnm, "invt"], writes=["bA0"])
        em.op("ve", lambda e, s, vb=vb, c=c: e.scalar_tensor_tensor(
            bA[0][:, 512:1024], P[1][:, :], invt[:, c:c + 1], vb[:, 512:1024],
            ALU.mult, ALU.mult),
            reads=["ps01", vnm], writes=["bA0"])
        em.op("ve", lambda e, s: e.tensor_tensor(bA[1][:], bA[3][:], bA[0][:], ALU.add),
              reads=["bA0", "bA3"], writes=["bA1"])
        rho3, bet3, snm3 = ln_stats(bA[1][:], "bA1", 24)
        em.op("ve", lambda e, s, rho3=rho3, bet3=bet3: e.tensor_scalar(
            bA[2][:], bA[1][:], rho3, bet3, ALU.mult, ALU.add),
            reads=[snm3 + "_rho", snm3 + "_bet", "bA1"], writes=["bA2"])
        split(bA[2][:], bH[3][:, 0:D], bH[3][:, D:2 * D], "bA2", "bH3")
        transp(bH[3], "bH3")
        evacT(bH[4], "bH4")
        mm3(P[2][:, :], P[3][:, :],
            lambda kc: bH[4][:, kc * 128:(kc + 1) * 128],
            lambda kc: bH[4][:, D + kc * 128:D + (kc + 1) * 128],
            FWh, FWl, ["bH4_h", "bH4_l", "wslot1"], "ps23")
        em.op("ve", lambda e, s: e.tensor_tensor(bA[2][:, 0:512], P[2][:, :],
                                                 bf_rep[:, 0:512], ALU.add),
              reads=["ps23", "bf_rep"], writes=["bA2"])
        em.op("ve", lambda e, s: e.tensor_tensor(bA[2][:, 512:1024], P[3][:, :],
                                                 bf_rep[:, 512:1024], ALU.add),
              reads=["ps23"], writes=["bA2"])
        em.op("ve", lambda e, s: e.tensor_tensor(bA[0][:], bA[2][:], bA[1][:], ALU.add),
              reads=["bA2", "bA1"], writes=["bA0"])
        dma("se", out_t[c * T:(c + 1) * T, :], bA[0][:], ["bA0"], [])

    final = {s: em.cnt[s] for s in Em.SEMS}

    with nc.Block() as block:
        sems = sem_handles

        @block.tensor
        def _(tensor):
            for f in em.ops["te"]:
                f(tensor, sems)

        @block.vector
        def _(vector):
            for f in em.ops["ve"]:
                f(vector, sems)

        @block.scalar
        def _(scalar):
            for f in em.ops["se"]:
                f(scalar, sems)
            for s_ in Em.DMA_SEMS["se"]:
                scalar.wait_ge(sems[s_], final[s_])

        @block.gpsimd
        def _(gpsimd):
            for f in em.ops["gp"]:
                f(gpsimd, sems)
            for s_ in Em.DMA_SEMS["se"] + Em.DMA_SEMS["gp"]:
                gpsimd.wait_ge(sems[s_], final[s_])

        @block.sync
        def _(sync):
            for f in em.ops["sy"]:
                f(sync, sems)
            for s_ in Em.DMA_SEMS["sy"]:
                sync.wait_ge(sems[s_], final[s_])

    es.close()
    return nc


_NC_CACHE = {}


def _get_program(ncores):
    if ncores not in _NC_CACHE:
        _NC_CACHE[ncores] = build_program(ncores)
    return _NC_CACHE[ncores]


def make_inputs(x, Wq, bq, Wqa, bqa, Wk, bk, Wka, bka, Wv, bv, gamma, beta,
                G1_R, G2_R, G1_I, G2_I, bias_real, bias_imag, ncores=8):
    B, S, Dm = x.shape

    def w2(a):
        return np.ascontiguousarray(np.asarray(a, np.float32))

    def row(a):
        return np.ascontiguousarray(np.asarray(a, np.float32).reshape(1, -1))

    def fm(a):
        return np.ascontiguousarray(np.asarray(a, np.float32).reshape(KC, 128).T)

    base = {
        "wq": w2(Wq), "wk": w2(Wk), "wv": w2(Wv), "wqa": w2(Wqa), "wka": w2(Wka),
        "bq": row(bq), "bk": row(bk), "bv": row(bv), "bqa": row(bqa), "bka": row(bka),
        "bffn": np.ascontiguousarray(np.concatenate([row(bias_real), row(bias_imag)], 1)),
        "gamma": fm(gamma), "beta": fm(beta),
    }
    g1r = np.transpose(np.asarray(G1_R, np.float32)[0], (2, 1, 0))
    g1i = np.transpose(np.asarray(G1_I, np.float32)[0], (2, 1, 0))
    base["g1"] = np.ascontiguousarray(
        np.concatenate([g1r, g1i], axis=1).reshape(32, 256))
    g2r = np.transpose(np.asarray(G2_R, np.float32)[:, :, 0, :], (2, 0, 1))
    g2i = np.transpose(np.asarray(G2_I, np.float32)[:, :, 0, :], (2, 0, 1))
    base["g2"] = np.ascontiguousarray(
        np.concatenate([g2r, g2i], axis=1).reshape(16, 128))

    base["eye"] = np.eye(128, dtype=np.float16)
    base["tri"] = np.triu(np.ones((128, 128), np.float16))
    base["ltris"] = np.triu(np.ones((16, 16), np.float16), k=1)
    onesc = np.zeros((128, 130), np.float16)
    onesc[:, 0] = 1.0
    onesc[0, 1:129] = 1.0
    base["onesc"] = onesc
    udm = np.zeros((32, 512), np.float32)
    for m in range(4):
        for p in range(128):
            udm[8 * m + p // 16, m * 128 + p] = 1.0
    base["ud"] = udm
    u2m = np.zeros((16, 128), np.float32)
    for p in range(128):
        u2m[p % 16, p] = 1.0
    base["u2"] = u2m
    selc = np.zeros((16, NCH * 128), np.float16)
    for cc in range(NCH):
        selc[cc, cc * 128:(cc + 1) * 128] = 1.0
    base["selc"] = selc

    in_maps = []
    for c in range(ncores):
        b, h = c // 2, c % 2
        im = dict(base)
        im["x"] = np.ascontiguousarray(x[b, h * SEQH:(h + 1) * SEQH, :])
        pos = np.arange(h * SEQH + 1, (h + 1) * SEQH + 1, dtype=np.float64)
        im["invt"] = np.ascontiguousarray((1.0 / pos).astype(np.float32).reshape(NCH, 128).T)
        mk = np.zeros((8, 16), np.float16)
        if h == 1:
            mk[c - 1, :] = 1.0
        im["maskc"] = mk
        in_maps.append(im)
    return in_maps


def kernel(L0_attn_hidden_states, L0_attn_attention_mask,
           L0_attn_Wq, L0_attn_bq, L0_attn_Wqa, L0_attn_bqa,
           L0_attn_Wk, L0_attn_bk, L0_attn_Wka, L0_attn_bka,
           L0_attn_Wv, L0_attn_bv, L0_attn_gamma, L0_attn_beta,
           L0_ffn_G1_R, L0_ffn_G2_R, L0_ffn_G1_I, L0_ffn_G2_I,
           L0_ffn_bias_real, L0_ffn_bias_imag):
    ncores = 8
    x = np.ascontiguousarray(np.asarray(L0_attn_hidden_states, np.float32))
    B, S, Dm = x.shape
    in_maps = make_inputs(
        x, L0_attn_Wq, L0_attn_bq, L0_attn_Wqa, L0_attn_bqa,
        L0_attn_Wk, L0_attn_bk, L0_attn_Wka, L0_attn_bka,
        L0_attn_Wv, L0_attn_bv, L0_attn_gamma, L0_attn_beta,
        L0_ffn_G1_R, L0_ffn_G2_R, L0_ffn_G1_I, L0_ffn_G2_I,
        L0_ffn_bias_real, L0_ffn_bias_imag, ncores)
    nc = _get_program(ncores)
    res = run_bass_kernel_spmd(nc, in_maps, core_ids=list(range(ncores)))
    out = np.empty((B, S, Dm), np.float32)
    for c in range(ncores):
        b, h = c // 2, c % 2
        out[b, h * SEQH:(h + 1) * SEQH, :] = res.results[c]["out"]
    return out
